# version 26 (final)
# baseline (speedup 1.0000x reference)
import numpy as np

import concourse.bass as bass
import concourse.mybir as mybir
from concourse.bass_utils import run_bass_kernel_spmd

F32 = mybir.dt.float32
BF16 = mybir.dt.bfloat16
AF = mybir.ActivationFunctionType
ALU = mybir.AluOpType
AX = mybir.AxisListType

MAGIC = 12582912.0

D = 2048
I = 8192
TOK = 2048
W = 512
NWAVE = TOK // W
DC = D // 128
OC = I // 128
NSLICE = W // 128
DIN = 512
CIN = DIN // 128
PO = 9
H = 32
RING = 5

LN_EPS = 1e-5
RMS_EPS = 1e-6
Q_EPS = 1e-5


class Sched:

    CSEMS = ("tensor", "vector", "scalar", "gpsimd")
    QSEMS = ("sync", "scalar", "gpsimd")

    def __init__(self):
        self.ops = {e: [] for e in ("tensor", "vector", "scalar", "gpsimd", "sync")}
        self.cnt = {}
        for e in self.CSEMS:
            self.cnt[("c", e)] = 0
        for q in self.QSEMS:
            self.cnt[("d", q)] = 0
        self.lw = {}
        self.readers = {}

    def _waits(self, selfkey, reads, writes):
        need = {}

        def add(tok):
            if tok is None:
                return
            k, v = tok
            if k == selfkey and k[0] == "c":
                return
            if need.get(k, -1) < v:
                need[k] = v

        for b in reads:
            add(self.lw.get(b))
        for b in writes:
            add(self.lw.get(b))
            for t in self.readers.get(b, ()):
                add(t)
        return need

    def _record(self, tok, reads, writes):
        for b in reads:
            self.readers.setdefault(b, []).append(tok)
        for b in writes:
            self.lw[b] = tok
            self.readers[b] = []

    def emit(self, eng, fn, reads=(), writes=(), tag=""):
        key = ("c", eng)
        waits = self._waits(key, reads, writes)
        self.cnt[key] += 1
        tok = (key, self.cnt[key])
        self.ops[eng].append((waits, fn, key, 1, tag))
        self._record(tok, reads, writes)
        return tok

    def dma(self, queue, fn, reads=(), writes=(), tag=""):
        key = ("d", queue)
        waits = self._waits(key, reads, writes)
        self.cnt[key] += 1
        tok = (key, 16 * self.cnt[key])
        self.ops[queue].append((waits, fn, key, 16, tag))
        self._record(tok, reads, writes)
        return tok

    def barrier_all(self, fn):
        waits = {}
        for k, v in self.cnt.items():
            if v:
                waits[k] = v if k[0] == "c" else 16 * v
        self.ops["sync"].append((waits, fn, None, 0, "final"))


def build_program(debug=False, waves=NWAVE, light=False):
    nc = bass.Bass("TRN2", target_bir_lowering=False)

    xT = nc.dram_tensor("xT", [D, TOK], F32, kind="ExternalInput").ap()
    wgT = nc.dram_tensor("wgT", [D, I], F32, kind="ExternalInput").ap()
    wuT = nc.dram_tensor("wuT", [D, I], F32, kind="ExternalInput").ap()
    wdT = nc.dram_tensor("wdT", [I, D], F32, kind="ExternalInput").ap()
    wxT4 = nc.dram_tensor("wxT4", [CIN, 128, DC * 128], F32, kind="ExternalInput").ap()
    wzT4 = nc.dram_tensor("wzT4", [CIN, 128, DC * 128], F32, kind="ExternalInput").ap()
    wpS = nc.dram_tensor("wpS", [3, 128, 2048], F32, kind="ExternalInput").ap()
    wdtT = nc.dram_tensor("wdtT", [128, H], F32, kind="ExternalInput").ap()
    woS = nc.dram_tensor("woS", [4, 128, 2048], F32, kind="ExternalInput").ap()
    convw = nc.dram_tensor("convw", [DIN, 4], F32, kind="ExternalInput").ap()
    convb = nc.dram_tensor("convb", [DIN], F32, kind="ExternalInput").ap()
    Dvec = nc.dram_tensor("Dvec", [DIN], F32, kind="ExternalInput").ap()
    bdt_in = nc.dram_tensor("bdt", [H], F32, kind="ExternalInput").ap()
    alog_in = nc.dram_tensor("Alog", [H, 16], F32, kind="ExternalInput").ap()
    lnw_in = nc.dram_tensor("lnw", [I], F32, kind="ExternalInput").ap()
    identI = nc.dram_tensor("identI", [128, 128], F32, kind="ExternalInput").ap()
    oh128I = nc.dram_tensor("oh128I", [128, 4 * 128], F32, kind="ExternalInput").ap()
    Eb4I = nc.dram_tensor("Eb4I", [H, CIN * 128], F32, kind="ExternalInput").ap()
    yT = nc.dram_tensor("yT", [D, TOK], F32, kind="ExternalOutput").ap()
    if debug:
        dq = nc.dram_tensor("dq", [128, DC * W], BF16, kind="ExternalOutput").ap()
        dv = nc.dram_tensor("dv", [128, OC * W], BF16, kind="ExternalOutput").ap()
        dr1 = nc.dram_tensor("dr1", [128, DC * W], F32, kind="ExternalOutput").ap()
        drm = nc.dram_tensor("drm", [128, DC * W], F32, kind="ExternalOutput").ap()
        dms = nc.dram_tensor("dms", [128, 8 * W], F32, kind="ExternalOutput").ap()
        dpsum = nc.dram_tensor("dpsum", [128, W], F32, kind="ExternalOutput").ap()
        dscr = nc.dram_tensor("dscr", [128, W], F32, kind="ExternalOutput").ap()
        dv2 = nc.dram_tensor("dv2", [128, OC * W], BF16, kind="ExternalOutput").ap()
        dc4 = nc.dram_tensor("dc4", [128, 16], F32, kind="ExternalOutput").ap()
        dext = nc.dram_tensor("dext", [128, 64], F32, kind="ExternalOutput").ap()
        dqg = nc.dram_tensor("dqg", [OC, 128, DC * 128], BF16, kind="ExternalOutput").ap()
        dqd = nc.dram_tensor("dqd", [DC, 128, OC * 128], BF16, kind="ExternalOutput").ap()

    qg3 = nc.dram_tensor("qg3", [OC, 128, DC * 128], BF16, kind="Internal").ap()
    qu3 = nc.dram_tensor("qu3", [OC, 128, DC * 128], BF16, kind="Internal").ap()
    qd3 = nc.dram_tensor("qd3", [DC, 128, OC * 128], BF16, kind="Internal").ap()

    A = nc.alloc_sbuf_tensor
    Rt = A("Rt", [128, DC * W], F32)
    Vt = A("Vt", [128, OC * W], BF16)
    qxT = A("qxT", [128, DC * W], BF16)
    tails = A("tails", [128, 4 * 32], F32)
    hstate = A("hstate", [128, 4 * 32], F32)
    ring = [A(f"ring{i}", [128, 2048], BF16) for i in range(RING)]
    xp = A("xp", [128, CIN * W], F32)
    zs_bf = A("zs_bf", [128, CIN * W], BF16)
    xa_bf = A("xa_bf", [128, CIN * W], BF16)
    par_bf = A("par_bf", [128, PO * W], BF16)
    y_bf = A("y_bf", [128, CIN * W], BF16)
    ext = A("ext", [128, W + 32], F32)
    t0_32 = A("t0_32", [H, W], F32)
    t2_32 = A("t2_32", [H, W], F32)
    delta = A("delta", [H, W], F32)

    sum_acc = A("sum_acc", [128, W], F32)
    ssq_acc = A("ssq_acc", [128, W], F32)
    rp = [A(f"rp{i}", [128, W], F32) for i in range(4)]
    scr_a = A("scr_a", [128, W], F32)
    scr_b = A("scr_b", [128, W], F32)
    scr_c = A("scr_c", [128, W], F32)
    scr_d = A("scr_d", [128, W], BF16)
    statw = A("statw", [128, 128], F32)
    stwT = A("stwT", [128, 128], F32)
    c4a = A("c4a", [128, 4], F32)
    c4b = A("c4b", [128, 4], F32)
    c4c = A("c4c", [128, 4], F32)
    c4d = A("c4d", [128, 4], F32)
    pf = [A(f"pf{i}", [128, 512], F32) for i in range(2)]
    qpf = [A(f"qpf{i}", [128, 512], BF16) for i in range(2)]
    pcol = A("pcol", [128, 1], F32)
    acc0 = A("acc0", [128, 1], F32)
    ident = A("ident", [128, 128], F32)
    ones_ln = A("ones_ln", [128, 128], F32)
    ones_w = A("ones_w", [128, 128], F32)
    oh128 = A("oh128", [128, 512], F32)
    E4 = A("E4", [H, CIN * 128], F32)
    Eb4 = A("Eb4", [H, CIN * 128], F32)
    alog = A("alog_t", [H, 16], F32)
    expA = A("expA", [H, 16], F32)
    bdt = A("bdt_c", [H, 1], F32)
    w4 = A("w4", [128, 16], F32)
    cb = A("cb", [128, 4], F32)
    Dfm = A("Dfm", [128, 4], F32)
    lnw_fm = A("lnw_fm", [128, OC], F32)
    wdt_bf = A("wdt_bf", [128, H], BF16)
    wm = {m: A(f"wm_{m}", [128, 1], F32) for m in ("g", "u", "d")}
    CCV = {"lneps": LN_EPS, "rmseps": RMS_EPS, "one": 1.0, "negone": -1.0,
           "neghalf": -0.5, "magic": MAGIC, "negmagic": -MAGIC}
    cc = {k: A(f"cc_{k}", [128, 1], F32) for k in CCV}
    sw = {m: A(f"sw_{m}", [128, 1], F32) for m in ("g", "u", "d")}
    fence_buf = A("fence_buf", [1, 16], BF16)
    mag_col = A("mag_col", [128, 1], F32)
    thf_col = A("thf_col", [128, 1], F32)

    P = nc.alloc_psum_tensor
    pb_g = P("pb_g", [128, W], F32)
    pb_u = P("pb_u", [128, W], F32)
    pb_d0 = P("pb_d0", [128, W], F32)
    pb_d1 = P("pb_d1", [128, W], F32)
    pb_l0 = P("pb_l0", [128, W], F32)
    pb_l1 = P("pb_l1", [128, W], F32)
    pb_x = P("pb_x", [128, W], F32)
    pb_j = P("pb_j", [1, 2], F32)

    s = Sched()

    def Rc(dc):
        return Rt.ap()[:, W * dc:W * (dc + 1)]

    def Vc(ic):
        return Vt.ap()[:, W * ic:W * (ic + 1)]

    def qx(dc):
        return qxT.ap()[:, W * dc:W * (dc + 1)]

    def vec(fn, reads, writes, tag=""):
        s.emit("vector", fn, reads=reads, writes=writes, tag=tag)

    def act(fn, reads, writes, tag=""):
        s.emit("scalar", fn, reads=reads, writes=writes, tag=tag)

    def pe(fn, reads, writes, tag=""):
        s.emit("tensor", fn, reads=reads, writes=writes, tag=tag)

    def g_init(g):
        g.memset(ones_ln.ap(), 1.0 / D)
        g.memset(ones_w.ap(), 1.0)
        g.memset(hstate.ap(), 0.0)
        g.memset(tails.ap(), 0.0)
        for k, v in CCV.items():
            g.memset(cc[k].ap(), v)
        return g.memset(acc0.ap(), 0.0)

    s.emit("gpsimd", g_init,
           writes=["ident", "ones_ln", "ones_w", "oh128", "Eb4", "hstate", "acc0",
                   "cc"] + [f"tail{c}" for c in range(CIN)])

    def ld(dst_ap, src_ap, wbuf):
        s.dma("gpsimd", lambda g, d=dst_ap, sp=src_ap: g.dma_start(out=d, in_=sp),
              writes=[wbuf], tag="constld")

    ld(ident.ap(), identI, "ident")
    ld(oh128.ap(), oh128I, "oh128")
    ld(Eb4.ap(), Eb4I, "Eb4")
    ld(alog.ap(), alog_in, "alog")
    ld(bdt.ap(), bdt_in.unsqueeze(1), "bdt")
    for k in range(CIN):
        ld(w4.ap()[:, 4 * k:4 * (k + 1)], convw[128 * k:128 * (k + 1), :], "w4")
        ld(cb.ap()[:, k:k + 1], convb[128 * k:128 * (k + 1)].unsqueeze(1), "cb")
        ld(Dfm.ap()[:, k:k + 1], Dvec[128 * k:128 * (k + 1)].unsqueeze(1), "Dfm")
    for oc in range(OC):
        ld(lnw_fm.ap()[:, oc:oc + 1], lnw_in[128 * oc:128 * (oc + 1)].unsqueeze(1), "lnw")
    s.dma("gpsimd", lambda g: g.dma_start(out=wdt_bf.ap(), in_=wdtT), writes=["wdt_bf"])

    act(lambda e: e.activation(out=expA.ap(), in_=alog.ap(), func=AF.Exp),
        ["alog"], ["expA"])

    def mk_E(v):
        ea = expA.ap().unsqueeze(1).to_broadcast([H, (CIN * 128) // 16, 16])
        return v.tensor_tensor(
            out=E4.ap().rearrange("h (r n) -> h r n", n=16),
            in0=Eb4.ap().rearrange("h (r n) -> h r n", n=16),
            in1=ea, op=ALU.mult)

    vec(mk_E, ["Eb4", "expA"], ["E4"])

    def mk_mag(v):
        v.memset(rp[0].ap(), 1048576.0)
        v.memset(rp[1].ap(), 2.0 ** -19)
        for _ in range(40):
            v.tensor_scalar(out=scr_a.ap(), in0=rp[0].ap(), scalar1=1.0,
                            scalar2=None, op0=ALU.add)
            v.tensor_tensor(out=scr_b.ap(), in0=scr_a.ap(), in1=rp[0].ap(),
                            op=ALU.subtract)
            v.tensor_scalar(out=scr_c.ap(), in0=scr_b.ap(), scalar1=0.5,
                            scalar2=1.0, op0=ALU.is_ge, op1=ALU.add)
            v.tensor_tensor(out=rp[0].ap(), in0=rp[0].ap(), in1=scr_c.ap(),
                            op=ALU.mult)
            v.tensor_scalar(out=scr_a.ap(), in0=scr_c.ap(), scalar1=0.5,
                            scalar2=None, op0=ALU.mult)
            v.tensor_tensor(out=rp[1].ap(), in0=rp[1].ap(), in1=scr_a.ap(),
                            op=ALU.mult)
        v.tensor_copy(out=stwT.ap()[:, 0:128], in_=ident.ap())
        v.tensor_copy(out=stwT.ap()[:, 0:128], in_=ident.ap())
        v.tensor_scalar(out=mag_col.ap(), in0=rp[0].ap()[:, 0:1], scalar1=0.75,
                        scalar2=None, op0=ALU.mult)
        v.tensor_copy(out=stwT.ap()[:, 0:128], in_=ident.ap())
        v.tensor_copy(out=stwT.ap()[:, 0:128], in_=ident.ap())
        v.tensor_scalar(out=thf_col.ap(), in0=rp[1].ap()[:, 0:1], scalar1=-3.0,
                        scalar2=1.0, op0=ALU.mult, op1=ALU.add)
        v.tensor_copy(out=stwT.ap()[:, 0:128], in_=ident.ap())
        return v.tensor_copy(out=stwT.ap()[:, 0:128], in_=ident.ap())

    vec(mk_mag, ["ident"], ["mag", "thf", "rp0", "rp1", "scr_a", "scr_b",
                            "scr_c", "stwT"])

    def phase0(mat, srcT, rows, cols, scratch):
        nel = rows * cols
        PC = 512
        nr, nq = rows // 128, cols // PC
        t = f"p0{mat}"
        pieces = [(r, q) for r in range(nr) for q in range(nq)]
        NP = len(pieces)
        for n, (r, q) in enumerate(pieces):
            p = n % 2
            src = srcT[128 * r:128 * (r + 1), PC * q:PC * (q + 1)]
            s.dma("scalar", lambda e, d=pf[p].ap(), sp=src: e.dma_start(out=d, in_=sp),
                  writes=[f"pf{p}"], tag=t)
            act(lambda e, p=p, n=n: e.activation(
                out=rp[1 + 2 * p].ap(), in_=pf[p].ap(), func=AF.Abs,
                accum_out=scr_a.ap()[:, n:n + 1]),
                [f"pf{p}"], ["scr_a", f"rp{1 + 2 * p}"], t)
        pe(lambda T, NP=NP: T.matmul(pb_x.ap()[:, 0:NP], ones_w.ap(),
                                     scr_a.ap()[:, 0:NP], start=True, stop=True),
           ["scr_a", "ones_w"], ["pbx"], t)

        def racc(v, NP=NP):
            for g16 in range(16):
                v.tensor_reduce(out=rp[1].ap()[:, g16:g16 + 1],
                                in_=pb_x.ap()[:, 16 * g16:16 * (g16 + 1)],
                                axis=AX.X, op=ALU.add)
            rflush(v)
            v.tensor_reduce(out=acc0.ap(), in_=rp[1].ap()[:, 0:16],
                            axis=AX.X, op=ALU.add)
            return rflush(v)

        vec(racc, ["pbx"], ["acc0", "rp1", "stwT"], t)
        vec(lambda v, ninv=1.0 / float(nel): v.tensor_scalar(
            out=wm[mat].ap(), in0=acc0.ap(), scalar1=ninv,
            scalar2=Q_EPS, op0=ALU.mult, op1=ALU.max),
            ["acc0"], [f"wm_{mat}"], t)
        def mkth(v, m=mat):
            v.tensor_copy(out=stwT.ap()[:, 0:128], in_=ident.ap())
            v.tensor_copy(out=stwT.ap()[:, 0:128], in_=ident.ap())
            v.tensor_scalar(out=sw[m].ap(), in0=wm[m].ap(), scalar1=0.5,
                            scalar2=None, op0=ALU.mult)
            v.tensor_copy(out=stwT.ap()[:, 0:128], in_=ident.ap())
            v.tensor_copy(out=stwT.ap()[:, 0:128], in_=ident.ap())
            v.tensor_tensor(out=sw[m].ap(), in0=sw[m].ap(), in1=thf_col.ap(),
                            op=ALU.mult)
            v.tensor_copy(out=stwT.ap()[:, 0:128], in_=ident.ap())
            v.tensor_copy(out=stwT.ap()[:, 0:128], in_=ident.ap())
            v.tensor_scalar(out=pcol.ap(), in0=sw[m].ap(), scalar1=-1.0,
                            scalar2=None, op0=ALU.mult)
            v.tensor_copy(out=stwT.ap()[:, 0:128], in_=ident.ap())
            return v.tensor_copy(out=stwT.ap()[:, 0:128], in_=ident.ap())

        vec(mkth, [f"wm_{mat}", "ident", "thf"], [f"sw_{mat}", "pcol", "stwT"], t)
        n = 0
        for q in range(nq):
            for r in range(nr):
                p = n % 2
                src = srcT[128 * r:128 * (r + 1), PC * q:PC * (q + 1)]
                s.dma("scalar", lambda e, d=pf[p].ap(), sp=src: e.dma_start(out=d, in_=sp),
                      writes=[f"pf{p}"], tag=t)
                vec(lambda v, p=p: v.tensor_tensor(
                    out=rp[2].ap(), in0=pf[p].ap(),
                    in1=pcol.ap().to_broadcast([128, PC]),
                    op=ALU.is_lt), [f"pf{p}", "pcol"], ["rp2"], t)
                vec(lambda v, p=p, m=mat: v.scalar_tensor_tensor(
                    out=qpf[p].ap(), in0=pf[p].ap(), scalar=sw[m].ap(),
                    in1=rp[2].ap(), op0=ALU.is_gt, op1=ALU.subtract),
                    [f"pf{p}", "rp2", f"sw_{mat}"], [f"qpf{p}"], t)
                nj = PC // 128
                dst = scratch[nj * q:nj * (q + 1), :, 128 * r:128 * (r + 1)]
                dst = dst.rearrange("j p c -> p j c")
                srcv = qpf[p].ap().rearrange("p (j c) -> p j c", c=128)
                s.dma("scalar", lambda e, d=dst, sp=srcv: e.dma_start(out=d, in_=sp),
                      reads=[f"qpf{p}"], writes=[f"q{mat}_h{q}"], tag=t)
                n += 1

    phase0("g", wgT, D, I, qg3)
    phase0("u", wuT, D, I, qu3)
    phase0("d", wdT, I, D, qd3)
    if debug:
        for q_ in range(16):
            s.dma("gpsimd", lambda g, d=dqg[4*q_:4*(q_+1)], sp=qg3[4*q_:4*(q_+1)]:
                  g.dma_start(out=d, in_=sp), reads=[f"qg_h{q_}"], tag="dbg")
        for q_ in range(4):
            s.dma("gpsimd", lambda g, d=dqd[4*q_:4*(q_+1)], sp=qd3[4*q_:4*(q_+1)]:
                  g.dma_start(out=d, in_=sp), reads=[f"qd_h{q_}"], tag="dbg")

    ring_state = {"i": 0}

    def next_slot():
        i = ring_state["i"] % RING
        ring_state["i"] += 1
        return i

    sq_state = {"n": 0}

    def load_slot_hw(srcv, readbufs, tag):
        sl = next_slot()
        n = sq_state["n"]
        sq_state["n"] += 1
        s.dma("sync", lambda e, d=ring[sl].ap(), sp=srcv: e.dma_start(out=d, in_=sp),
              reads=list(readbufs), writes=[f"ring{sl}", f"SQ{n}"], tag=tag)
        return sl, n

    def sync_fence(tag):
        n = sq_state["n"]
        sq_state["n"] += 1
        s.dma("sync", lambda e: e.dma_start(out=fence_buf.ap(), in_=qg3[0][0:1, 0:16]),
              writes=[f"SQ{n}"], tag=tag)
        return n

    gq_state = {"n": 0}

    def load_slot_cast(srcv, tag):
        sl = next_slot()
        n = gq_state["n"]
        gq_state["n"] += 1
        s.dma("gpsimd", lambda g, d=ring[sl].ap(), sp=srcv: g.dma_start(out=d, in_=sp),
              writes=[f"ring{sl}", f"GQ{n}"], tag=tag)
        return sl, n

    def gq_fence(tag):
        n = gq_state["n"]
        gq_state["n"] += 1
        s.dma("gpsimd", lambda g: g.dma_start(out=fence_buf.ap(),
                                              in_=qg3[0][0:1, 0:16]),
              writes=[f"GQ{n}"], tag=tag)
        return n

    def tile(sl, j):
        return ring[sl].ap()[:, 128 * j:128 * (j + 1)]

    def layernorm(t, mamba_out):
        for dc in range(DC):
            if dc == 0:
                vec(lambda v: v.tensor_copy(out=sum_acc.ap(), in_=Rc(0)),
                    ["R0"], ["sum_acc"], t)
            else:
                vec(lambda v, dc=dc: v.tensor_tensor(out=sum_acc.ap(), in0=sum_acc.ap(),
                                                     in1=Rc(dc), op=ALU.add),
                    [f"R{dc}"], ["sum_acc"], t)
            act(lambda e, dc=dc: e.activation(out=scr_c.ap(), in_=Rc(dc),
                                              func=AF.Square),
                [f"R{dc}"], ["scr_c"], t)
            if dc == 0:
                vec(lambda v: v.tensor_copy(out=ssq_acc.ap(), in_=scr_c.ap()),
                    ["scr_c"], ["ssq_acc"], t)
            else:
                vec(lambda v: v.tensor_tensor(out=ssq_acc.ap(), in0=ssq_acc.ap(),
                                              in1=scr_c.ap(), op=ALU.add),
                    ["scr_c"], ["ssq_acc"], t)
        pe(lambda T: T.matmul(pb_l0.ap(), ones_ln.ap(), sum_acc.ap(),
                              start=True, stop=True),
           ["sum_acc", "ones_ln"], ["pbl0"], t)
        pe(lambda T: T.matmul(pb_l1.ap(), ones_ln.ap(), ssq_acc.ap(),
                              start=True, stop=True),
           ["ssq_acc", "ones_ln"], ["pbl1"], t)
        vec(lambda v: v.tensor_copy(out=rp[3].ap(), in_=pb_l0.ap()), ["pbl0"], ["rp3"], t)
        vec(lambda v: v.tensor_tensor(out=scr_c.ap(), in0=rp[3].ap(), in1=rp[3].ap(),
                                      op=ALU.mult), ["rp3"], ["scr_c"], t)
        vec(lambda v: v.tensor_tensor(out=scr_c.ap(), in0=pb_l1.ap(), in1=scr_c.ap(),
                                      op=ALU.subtract), ["pbl1", "scr_c"], ["scr_c"], t)
        act(lambda e: e.activation(out=scr_c.ap(), in_=scr_c.ap(), func=AF.Ln,
                                   bias=cc["lneps"].ap()), ["scr_c", "cc"], ["scr_c"], t)
        act(lambda e: e.activation(out=rp[0].ap(), in_=scr_c.ap(), func=AF.Exp,
                                   scale=cc["neghalf"].ap()), ["scr_c", "cc"], ["rp0"], t)
        vec(lambda v: v.tensor_tensor(out=rp[1].ap(), in0=rp[3].ap(), in1=rp[0].ap(),
                                      op=ALU.mult), ["rp3", "rp0"], ["rp1"], t)
        if mamba_out:
            for dc in range(DC):
                vec(lambda v, dc=dc: v.tensor_tensor(out=scr_a.ap(), in0=Rc(dc),
                                                     in1=rp[0].ap(), op=ALU.mult),
                    [f"R{dc}", "rp0"], ["scr_a"], t)
                vec(lambda v, dc=dc: v.tensor_tensor(out=qx(dc), in0=scr_a.ap(),
                                                     in1=rp[1].ap(), op=ALU.subtract),
                    ["scr_a", "rp1"], [f"qx{dc}"], t)

    def rflush(v):
        v.tensor_copy(out=stwT.ap()[:, 0:128], in_=ident.ap())
        v.tensor_copy(out=stwT.ap()[:, 0:128], in_=ident.ap())
        v.tensor_copy(out=stwT.ap()[:, 0:128], in_=ident.ap())
        return v.tensor_copy(out=stwT.ap()[:, 0:128], in_=ident.ap())

    def colstats(acc_name, acc_t, out4, out4_name, op, t):
        def tps(T):
            for sl in range(NSLICE):
                T.transpose(pb_x.ap()[:, 128 * sl:128 * (sl + 1)],
                            acc_t.ap()[:, 128 * sl:128 * (sl + 1)], ident.ap())
            return T.matmul(pb_j.ap()[0:1, 0:1], ones_w.ap()[0:1, 0:1],
                            ones_w.ap()[0:1, 0:1], start=True, stop=True)

        pe(tps, [acc_name, "ident", "ones_w"], ["pbx", "pbj"], t)

        def red(v):
            for sl in range(NSLICE):
                v.tensor_reduce(out=statw.ap()[:, sl:sl + 1],
                                in_=pb_x.ap()[:, 128 * sl:128 * (sl + 1)],
                                axis=AX.X, op=op)
            return rflush(v)

        vec(red, ["pbx"], ["statw", "stwT"], t)

    def bcast_stat(dst, dst_name, pb, pbn, t):
        def tpw(T):
            T.transpose(pb_x.ap()[:, 0:128], statw.ap(), ident.ap())
            return T.matmul(pb_j.ap()[0:1, 0:1], ones_w.ap()[0:1, 0:1],
                            ones_w.ap()[0:1, 0:1], start=True, stop=True)

        pe(tpw, ["statw", "ident", "ones_w"], ["pbx", "pbj"], t)
        vec(lambda v: v.tensor_copy(out=stwT.ap(), in_=pb_x.ap()[:, 0:128]),
            ["pbx"], ["stwT"], t)

        def mmA(T):
            ins = None
            for sl in range(4):
                ins = T.matmul(pb.ap()[:, 128 * sl:128 * (sl + 1)],
                               oh128.ap()[:, 128 * sl:128 * (sl + 1)], stwT.ap(),
                               start=True, stop=True)
            return ins

        pe(mmA, ["stwT", "oh128"], [pbn], t)
        vec(lambda v: v.tensor_copy(out=dst.ap(), in_=pb.ap()), [pbn], [dst_name], t)

    def ffn(w_, call):
        t = f"w{w_}f{call}"
        layernorm(t, mamba_out=False)
        for dc in range(DC):
            vec(lambda v, dc=dc: v.tensor_tensor(out=scr_a.ap(), in0=Rc(dc),
                                                 in1=rp[0].ap(), op=ALU.mult),
                [f"R{dc}", "rp0"], ["scr_a"], t)
            vec(lambda v: v.tensor_tensor(out=scr_b.ap(), in0=scr_a.ap(),
                                          in1=rp[1].ap(), op=ALU.subtract),
                ["scr_a", "rp1"], ["scr_b"], t)
            act(lambda e: e.activation(out=scr_a.ap(), in_=scr_b.ap(), func=AF.Abs),
                ["scr_b"], ["scr_a"], t)
            if dc == 0:
                vec(lambda v: v.tensor_copy(out=sum_acc.ap(), in_=scr_a.ap()),
                    ["scr_a"], ["sum_acc"], t)
            else:
                vec(lambda v: v.tensor_tensor(out=sum_acc.ap(), in0=sum_acc.ap(),
                                              in1=scr_a.ap(), op=ALU.max),
                    ["scr_a"], ["sum_acc"], t)
        colstats("sum_acc", sum_acc, c4a, "c4a", ALU.max, t)
        bcast_stat(rp[2], "rp2", pb_d0, "pbd0", t)
        vec(lambda v: v.tensor_scalar(out=rp[3].ap(), in0=rp[2].ap(), scalar1=Q_EPS,
                                      scalar2=None, op0=ALU.max), ["rp2"], ["rp3"], t)
        act(lambda e: e.activation(out=scr_c.ap(), in_=rp[3].ap(), func=AF.Ln),
            ["rp3"], ["scr_c"], t)
        act(lambda e: e.activation(out=scr_c.ap(), in_=scr_c.ap(), func=AF.Exp,
                                   scale=cc["negone"].ap()), ["scr_c", "cc"], ["scr_c"], t)
        vec(lambda v: v.tensor_scalar(out=rp[2].ap(), in0=scr_c.ap(), scalar1=127.0,
                                      scalar2=None, op0=ALU.mult), ["scr_c"], ["rp2"], t)
        for dc in range(DC):
            vec(lambda v, dc=dc: v.tensor_tensor(out=scr_a.ap(), in0=Rc(dc),
                                                 in1=rp[0].ap(), op=ALU.mult),
                [f"R{dc}", "rp0"], ["scr_a"], t)
            vec(lambda v: v.tensor_tensor(out=scr_b.ap(), in0=scr_a.ap(),
                                          in1=rp[1].ap(), op=ALU.subtract),
                ["scr_a", "rp1"], ["scr_b"], t)
            vec(lambda v: v.tensor_tensor(out=scr_a.ap(), in0=scr_b.ap(),
                                          in1=rp[2].ap(), op=ALU.mult),
                ["scr_b", "rp2"], ["scr_a"], t)
            vec(lambda v: v.tensor_scalar(out=scr_a.ap(), in0=scr_a.ap(),
                                          scalar1=mag_col.ap(),
                                          scalar2=mag_col.ap(),
                                          op0=ALU.add, op1=ALU.subtract),
                ["scr_a", "mag"], ["scr_a"], t)
            vec(lambda v, dc=dc: v.tensor_copy(out=qx(dc), in_=scr_a.ap()),
                ["scr_a"], [f"qx{dc}"], t)
        if debug and t == "w0f1":
            s.dma("scalar", lambda e: e.dma_start(out=dq, in_=qxT.ap()),
                  reads=[f"qx{dc}" for dc in range(DC)], tag="dbg")
            s.dma("scalar", lambda e: e.dma_start(out=dms[:, 0:W], in_=rp[0].ap()),
                  reads=["rp0"], tag="dbg")
            s.dma("scalar", lambda e: e.dma_start(out=dms[:, W:2*W], in_=rp[1].ap()),
                  reads=["rp1"], tag="dbg")
            s.dma("scalar", lambda e: e.dma_start(out=dms[:, 2*W:3*W], in_=rp[2].ap()),
                  reads=["rp2"], tag="dbg")
            s.dma("scalar", lambda e: e.dma_start(out=dms[:, 3*W:4*W], in_=rp[3].ap()),
                  reads=["rp3"], tag="dbg")
        vec(lambda v: v.tensor_scalar(out=rp[0].ap(), in0=rp[3].ap(),
                                      scalar1=wm["g"].ap(), scalar2=1.0 / 127.0,
                                      op0=ALU.mult, op1=ALU.mult),
            ["rp3", "wm_g"], ["rp0"], t)
        vec(lambda v: v.tensor_scalar(out=rp[1].ap(), in0=rp[3].ap(),
                                      scalar1=wm["u"].ap(), scalar2=1.0 / 127.0,
                                      op0=ALU.mult, op1=ALU.mult),
            ["rp3", "wm_u"], ["rp1"], t)
        qx_all = [f"qx{dc}" for dc in range(DC)]

        def emit_gu(oc, sl_g, sl_u, n_u, n_gate_u):
            def mmg(T, sl=sl_g):
                ins = None
                for dc in range(DC):
                    ins = T.matmul(pb_g.ap(), tile(sl, dc), qx(dc),
                                   start=(dc == 0), stop=(dc == DC - 1))
                return ins

            def mmu(T, sl=sl_u):
                ins = None
                for dc in range(DC):
                    ins = T.matmul(pb_u.ap(), tile(sl, dc), qx(dc),
                                   start=(dc == 0), stop=(dc == DC - 1))
                return ins

            pe(mmg, [f"ring{sl_g}", f"SQ{n_u}"] + qx_all, ["pbg"], t)
            pe(mmu, [f"ring{sl_u}", f"SQ{n_gate_u}"] + qx_all, ["pbu"], t)
            vec(lambda v: v.tensor_tensor(out=scr_a.ap(), in0=pb_g.ap(), in1=rp[0].ap(),
                                          op=ALU.mult), ["pbg", "rp0"], ["scr_a"], t)
            act(lambda e: e.activation(out=scr_a.ap(), in_=scr_a.ap(), func=AF.Silu),
                ["scr_a"], ["scr_a"], t)
            vec(lambda v: v.tensor_tensor(out=scr_b.ap(), in0=pb_u.ap(), in1=rp[1].ap(),
                                          op=ALU.mult), ["pbu", "rp1"], ["scr_b"], t)
            vec(lambda v: v.tensor_tensor(out=scr_b.ap(), in0=scr_a.ap(),
                                          in1=scr_b.ap(), op=ALU.mult),
                ["scr_a", "scr_b"], ["scr_b"], t)
            act(lambda e: e.activation(out=scr_c.ap(), in_=scr_b.ap(), func=AF.Square),
                ["scr_b"], ["scr_c"], t)
            if oc == 0:
                vec(lambda v: v.tensor_copy(out=ssq_acc.ap(), in_=scr_c.ap()),
                    ["scr_c"], ["ssq_acc"], t)
            else:
                vec(lambda v: v.tensor_tensor(out=ssq_acc.ap(), in0=ssq_acc.ap(),
                                              in1=scr_c.ap(), op=ALU.add),
                    ["scr_c"], ["ssq_acc"], t)
            vec(lambda v, oc=oc: v.tensor_scalar(out=Vc(oc), in0=scr_b.ap(),
                                                 scalar1=lnw_fm.ap()[:, oc:oc + 1],
                                                 scalar2=None, op0=ALU.mult),
                ["scr_b", "lnw"], [f"V{oc}"], t)
            act(lambda e, oc=oc: e.activation(out=scr_d.ap(), in_=Vc(oc), func=AF.Abs),
                [f"V{oc}"], ["scr_d"], t)
            if oc == 0:
                vec(lambda v: v.tensor_copy(out=sum_acc.ap(), in_=scr_d.ap()),
                    ["scr_d"], ["sum_acc"], t)
            else:
                vec(lambda v: v.tensor_tensor(out=sum_acc.ap(), in0=sum_acc.ap(),
                                              in1=scr_d.ap(), op=ALU.max),
                    ["scr_d"], ["sum_acc"], t)

        pend = None
        for oc in range(OC):
            sl_g, n_g = load_slot_hw(qg3[oc], [f"qg_h{oc // 4}"], t)
            sl_u, n_u = load_slot_hw(qu3[oc], [f"qu_h{oc // 4}"], t)
            if pend is not None:
                emit_gu(*pend, n_gate_u=n_g)
            pend = (oc, sl_g, sl_u, n_u)
        emit_gu(*pend, n_gate_u=sync_fence(t))
        if debug and t == "w0f1":
            s.dma("scalar", lambda e: e.dma_start(out=dv, in_=Vt.ap()),
                  reads=[f"V{ic}" for ic in range(OC)], tag="dbg")
            s.dma("scalar", lambda e: e.dma_start(out=dms[:, 4*W:5*W], in_=ssq_acc.ap()),
                  reads=["ssq_acc"], tag="dbg")
            s.dma("scalar", lambda e: e.dma_start(out=dms[:, 5*W:6*W], in_=sum_acc.ap()),
                  reads=["sum_acc"], tag="dbg")
        colstats("ssq_acc", ssq_acc, c4a, "c4a", ALU.add, t)
        bcast_stat(rp[2], "rp2", pb_d0, "pbd0", t)
        colstats("sum_acc", sum_acc, c4b, "c4b", ALU.max, t)
        bcast_stat(rp[3], "rp3", pb_d1, "pbd1", t)
        act(lambda e: e.activation(out=scr_c.ap(), in_=rp[2].ap(), func=AF.Ln,
                                   scale=1.0 / I, bias=cc["rmseps"].ap()),
            ["rp2", "cc"], ["scr_c"], t)
        act(lambda e: e.activation(out=scr_b.ap(), in_=scr_c.ap(), func=AF.Exp,
                                   scale=cc["neghalf"].ap()), ["scr_c", "cc"], ["scr_b"], t)
        vec(lambda v: v.tensor_tensor(out=scr_a.ap(), in0=scr_b.ap(), in1=rp[3].ap(),
                                      op=ALU.mult), ["scr_b", "rp3"], ["scr_a"], t)
        vec(lambda v: v.tensor_scalar(out=scr_a.ap(), in0=scr_a.ap(), scalar1=Q_EPS,
                                      scalar2=None, op0=ALU.max), ["scr_a"], ["scr_a"], t)
        act(lambda e: e.activation(out=scr_c.ap(), in_=scr_a.ap(), func=AF.Ln),
            ["scr_a"], ["scr_c"], t)
        act(lambda e: e.activation(out=scr_c.ap(), in_=scr_c.ap(), func=AF.Exp,
                                   scale=cc["negone"].ap()), ["scr_c", "cc"], ["scr_c"], t)
        vec(lambda v: v.tensor_scalar(out=scr_c.ap(), in0=scr_c.ap(), scalar1=127.0,
                                      scalar2=None, op0=ALU.mult), ["scr_c"], ["scr_c"], t)
        vec(lambda v: v.tensor_tensor(out=rp[2].ap(), in0=scr_b.ap(), in1=scr_c.ap(),
                                      op=ALU.mult), ["scr_b", "scr_c"], ["rp2"], t)
        vec(lambda v: v.tensor_scalar(out=rp[3].ap(), in0=scr_a.ap(),
                                      scalar1=wm["d"].ap(), scalar2=0.5 / 127.0,
                                      op0=ALU.mult, op1=ALU.mult),
            ["scr_a", "wm_d"], ["rp3"], t)
        if debug and t == "w0f1":
            s.dma("scalar", lambda e: e.dma_start(out=dc4[:, 0:16], in_=statw.ap()[:, 0:16]),
                  reads=["statw"], tag="dbg")
            s.dma("scalar", lambda e: e.dma_start(out=dms[:, 6*W:7*W], in_=rp[2].ap()),
                  reads=["rp2"], tag="dbg")
            s.dma("scalar", lambda e: e.dma_start(out=dms[:, 7*W:8*W], in_=rp[3].ap()),
                  reads=["rp3"], tag="dbg")
        for ic in range(OC):
            vec(lambda v, ic=ic: v.tensor_tensor(out=scr_a.ap(), in0=Vc(ic),
                                                 in1=rp[2].ap(), op=ALU.mult),
                [f"V{ic}", "rp2"], ["scr_a"], t)
            vec(lambda v: v.tensor_scalar(out=scr_a.ap(), in0=scr_a.ap(),
                                          scalar1=mag_col.ap(),
                                          scalar2=mag_col.ap(),
                                          op0=ALU.add, op1=ALU.subtract),
                ["scr_a", "mag"], ["scr_a"], t)
            vec(lambda v, ic=ic: v.tensor_copy(out=Vc(ic), in_=scr_a.ap()),
                ["scr_a"], [f"V{ic}"], t)
        if debug and t == "w0f1":
            s.dma("scalar", lambda e: e.dma_start(out=dv2, in_=Vt.ap()),
                  reads=[f"V{ic}" for ic in range(OC)], tag="dbg")
        def emit_down_piece(dc, g4, sl, n_gate):
            pb, pbn = (pb_d0, "pbd0") if dc % 2 == 0 else (pb_d1, "pbd1")

            def mmd(T, sl=sl, g4=g4, pb=pb):
                ins = None
                for j in range(16):
                    ic = 16 * g4 + j
                    ins = T.matmul(pb.ap(), tile(sl, j), Vc(ic),
                                   start=(ic == 0), stop=(ic == OC - 1))
                return ins

            s.emit("tensor", mmd,
                   reads=[f"ring{sl}", f"SQ{n_gate}"] +
                         [f"V{16 * g4 + j}" for j in range(16)],
                   writes=[pbn], tag=t)
            if g4 == 3:
                if debug and t == "w0f1" and dc == 2:
                    vec(lambda v, pb=pb: v.tensor_copy(out=xp.ap()[:, 0:W],
                                                       in_=pb.ap()),
                        [pbn], ["xp0"], t)
                    s.dma("scalar", lambda e: e.dma_start(out=dpsum,
                                                          in_=xp.ap()[:, 0:W]),
                          reads=["xp0"], tag="dbg")
                vec(lambda v, pb=pb: v.tensor_tensor(out=scr_a.ap(), in0=pb.ap(),
                                                     in1=rp[3].ap(), op=ALU.mult),
                    [pbn, "rp3"], ["scr_a"], t)
                if debug and t == "w0f1" and dc == 2:
                    s.dma("scalar", lambda e: e.dma_start(out=dscr, in_=scr_a.ap()),
                          reads=["scr_a"], tag="dbg")
                vec(lambda v, dc=dc: v.tensor_tensor(out=Rc(dc), in0=Rc(dc),
                                                     in1=scr_a.ap(), op=ALU.add),
                    ["scr_a", f"R{dc}"], [f"R{dc}"], t)

        pendd = None
        for dc in range(DC):
            for g in range(4):
                sl, n = load_slot_hw(qd3[dc][:, 2048 * g:2048 * (g + 1)],
                                     [f"qd_h{dc // 4}"], t)
                if pendd is not None:
                    emit_down_piece(*pendd, n_gate=n)
                pendd = (dc, g, sl)
        emit_down_piece(*pendd, n_gate=sync_fence(t))

    def mamba(w_):
        t = f"w{w_}m"
        layernorm(t, mamba_out=True)
        xln_all = [f"qx{dc}" for dc in range(DC)]

        def emit_xz(pi, c, sl, n_gate):
            pb, pbn = (pb_g, "pbg") if pi == 0 else (pb_u, "pbu")

            def mmx(T, sl=sl, pb=pb):
                ins = None
                for dc in range(DC):
                    ins = T.matmul(pb.ap(), tile(sl, dc), qx(dc),
                                   start=(dc == 0), stop=(dc == DC - 1))
                return ins

            pe(mmx, [f"ring{sl}", f"GQ{n_gate}"] + xln_all, [pbn], t)
            if pi == 0:
                vec(lambda v, c=c, pb=pb, pbn=pbn: v.tensor_copy(
                    out=xp.ap()[:, W * c:W * (c + 1)], in_=pb.ap()),
                    [pbn], [f"xp{c}"], t)
            else:
                act(lambda e, c=c, pb=pb, pbn=pbn: e.activation(
                    out=zs_bf.ap()[:, W * c:W * (c + 1)], in_=pb.ap(),
                    func=AF.Silu), [pbn], [f"zs{c}"], t)

        xz_pend = None
        for pi in range(2):
            w4d = wxT4 if pi == 0 else wzT4
            for c in range(CIN):
                sl, n = load_slot_cast(w4d[c], t)
                if xz_pend is not None:
                    emit_xz(*xz_pend, n_gate=n)
                xz_pend = (pi, c, sl)
        if debug and t == "w1m":
            s.dma("scalar", lambda e: e.dma_start(out=dc4[:, 0:16],
                                                  in_=tails.ap()[:, 16:32]),
                  reads=[f"tail{c}" for c in range(CIN)], tag="dbg")
            s.dma("scalar", lambda e: e.dma_start(out=dms[:, 5*W:6*W],
                                                  in_=xp.ap()[:, 0:W]),
                  reads=["xp0"], tag="dbg")
        for c in range(CIN):
            vec(lambda v, c=c: v.tensor_copy(out=ext.ap()[:, 0:3],
                                             in_=tails.ap()[:, 3 * c:3 * (c + 1)]),
                [f"tail{c}"], ["ext"], t)
            vec(lambda v, c=c: v.tensor_copy(out=ext.ap()[:, 3:3 + W],
                                             in_=xp.ap()[:, W * c:W * (c + 1)]),
                [f"xp{c}"], ["ext"], t)
            if debug and t == "w0m" and c == 0:
                s.dma("scalar", lambda e: e.dma_start(out=dext[:, 0:16],
                                                      in_=ext.ap()[:, 504:520]),
                      reads=["ext"], tag="dbg")
                s.dma("scalar", lambda e: e.dma_start(out=dext[:, 16:32],
                                                      in_=xp.ap()[:, 496:512]),
                      reads=["xp0"], tag="dbg")
                s.dma("scalar", lambda e: e.dma_start(out=dext[:, 32:44],
                                                      in_=tails.ap()[:, 0:12]),
                      reads=["tail0"], tag="dbg")
            vec(lambda v, c=c: v.tensor_scalar(out=scr_a.ap(), in0=ext.ap()[:, 0:W],
                                               scalar1=w4.ap()[:, 4 * c:4 * c + 1],
                                               scalar2=None, op0=ALU.mult),
                ["ext", "w4"], ["scr_a"], t)
            for j in (1, 2, 3):
                src, dst = (scr_a, scr_b) if j % 2 == 1 else (scr_b, scr_a)
                vec(lambda v, c=c, j=j, src=src, dst=dst: v.scalar_tensor_tensor(
                    out=dst.ap(), in0=ext.ap()[:, j:j + W],
                    scalar=w4.ap()[:, 4 * c + j:4 * c + j + 1], in1=src.ap(),
                    op0=ALU.mult, op1=ALU.add),
                    ["ext", "w4", src.name], [dst.name], t)
            vec(lambda v, c=c: v.tensor_copy(out=tails.ap()[:, 3 * c:3 * (c + 1)],
                                             in_=ext.ap()[:, W:W + 3]),
                ["ext"], [f"tail{c}"], t)
            act(lambda e, c=c: e.activation(out=xp.ap()[:, W * c:W * (c + 1)],
                                            in_=scr_b.ap(), func=AF.Silu,
                                            bias=cb.ap()[:, c:c + 1]),
                ["scr_b", "cb"], [f"xp{c}"], t)
            vec(lambda v, c=c: v.tensor_copy(out=xa_bf.ap()[:, W * c:W * (c + 1)],
                                             in_=xp.ap()[:, W * c:W * (c + 1)]),
                [f"xp{c}"], [f"xa{c}"], t)
        slp_pairs = [load_slot_cast(wpS[k], t) for k in range(3)]
        slp = [p[0] for p in slp_pairs]
        emit_xz(*xz_pend, n_gate=slp_pairs[0][1])
        n_pf = gq_fence(t)
        slp_gate = [slp_pairs[1][1], slp_pairs[2][1], n_pf]
        xa_all = [f"xa{c}" for c in range(CIN)]
        for po in range(PO):
            def mmp(T, po=po):
                ins = None
                for c in range(CIN):
                    ti = po * CIN + c
                    ins = T.matmul(pb_l0.ap(), tile(slp[ti // 16], ti % 16),
                                   xa_bf.ap()[:, W * c:W * (c + 1)],
                                   start=(c == 0), stop=(c == CIN - 1))
                return ins

            pe(mmp, [f"ring{slp[(po * CIN) // 16]}",
                     f"ring{slp[(po * CIN + CIN - 1) // 16]}",
                     f"GQ{slp_gate[(po * CIN) // 16]}",
                     f"GQ{slp_gate[(po * CIN + CIN - 1) // 16]}"] + xa_all,
               ["pbl0"], t)
            vec(lambda v, po=po: v.tensor_copy(out=par_bf.ap()[:, W * po:W * (po + 1)],
                                               in_=pb_l0.ap()), ["pbl0"], [f"par{po}"], t)
        pe(lambda T: T.matmul(pb_l1.ap()[0:H, :], wdt_bf.ap(), par_bf.ap()[:, 0:W],
                              start=True, stop=True),
           ["par0", "wdt_bf"], ["pbl1"], t)
        vec(lambda v: v.tensor_scalar(out=t0_32.ap(), in0=pb_l1.ap()[0:H, :],
                                      scalar1=bdt.ap(), scalar2=None, op0=ALU.add),
            ["pbl1", "bdt"], ["t0_32"], t)
        act(lambda e: e.activation(out=t2_32.ap(), in_=t0_32.ap(), func=AF.Abs),
            ["t0_32"], ["t2_32"], t)
        act(lambda e: e.activation(out=t2_32.ap(), in_=t2_32.ap(), func=AF.Exp,
                                   scale=cc["negone"].ap()[0:H]), ["t2_32", "cc"], ["t2_32"], t)
        act(lambda e: e.activation(out=t2_32.ap(), in_=t2_32.ap(), func=AF.Ln,
                                   bias=cc["one"].ap()[0:H]), ["t2_32", "cc"], ["t2_32"], t)
        act(lambda e: e.activation(out=delta.ap(), in_=t0_32.ap(), func=AF.Relu),
            ["t0_32"], ["delta"], t)
        vec(lambda v: v.tensor_tensor(out=delta.ap(), in0=delta.ap(), in1=t2_32.ap(),
                                      op=ALU.add), ["t2_32"], ["delta"], t)
        for c in range(CIN):
            pe(lambda T, c=c: T.matmul(pb_l0.ap(), E4.ap()[:, 128 * c:128 * (c + 1)],
                                       delta.ap(), start=True, stop=True),
               ["delta", "E4"], ["pbl0"], t)
            act(lambda e: e.activation(out=scr_a.ap(), in_=pb_l0.ap(), func=AF.Exp,
                                       scale=cc["negone"].ap()), ["pbl0", "cc"], ["scr_a"], t)
            pe(lambda T, c=c: T.matmul(pb_l1.ap(), Eb4.ap()[:, 128 * c:128 * (c + 1)],
                                       delta.ap(), start=True, stop=True),
               ["delta", "Eb4"], ["pbl1"], t)
            vec(lambda v, c=c: v.tensor_tensor(
                out=scr_b.ap(), in0=pb_l1.ap(),
                in1=par_bf.ap()[:, W * (1 + c):W * (2 + c)], op=ALU.mult),
                ["pbl1", f"par{1 + c}"], ["scr_b"], t)
            vec(lambda v, c=c: v.tensor_tensor(out=scr_b.ap(), in0=scr_b.ap(),
                                               in1=xp.ap()[:, W * c:W * (c + 1)],
                                               op=ALU.mult), [f"xp{c}"], ["scr_b"], t)
            vec(lambda v, c=c: v.tensor_tensor_scan(
                out=scr_c.ap(), data0=scr_a.ap(), data1=scr_b.ap(),
                initial=hstate.ap()[:, 32 * c + 31:32 * (c + 1)],
                op0=ALU.mult, op1=ALU.add),
                ["scr_a", "scr_b", f"hst{c}"], ["scr_c"], t)
            vec(lambda v, c=c: v.tensor_tensor(
                out=scr_a.ap(), in0=scr_c.ap(),
                in1=par_bf.ap()[:, W * (5 + c):W * (6 + c)], op=ALU.mult),
                ["scr_c", f"par{5 + c}"], ["scr_a"], t)
            vec(lambda v, c=c: v.scalar_tensor_tensor(
                out=scr_b.ap(), in0=xp.ap()[:, W * c:W * (c + 1)],
                scalar=Dfm.ap()[:, c:c + 1], in1=scr_a.ap(), op0=ALU.mult, op1=ALU.add),
                [f"xp{c}", "Dfm", "scr_a"], ["scr_b"], t)
            vec(lambda v, c=c: v.tensor_copy(out=hstate.ap()[:, 32 * c:32 * (c + 1)],
                                             in_=scr_c.ap()[:, W - 32:W]),
                ["scr_c"], [f"hst{c}"], t)
            vec(lambda v, c=c: v.tensor_tensor(
                out=y_bf.ap()[:, W * c:W * (c + 1)], in0=scr_b.ap(),
                in1=zs_bf.ap()[:, W * c:W * (c + 1)], op=ALU.mult),
                ["scr_b", f"zs{c}"], [f"y{c}"], t)
        slo_pairs = [load_slot_cast(woS[k], t) for k in range(4)]
        slo = [p[0] for p in slo_pairs]
        n_of = gq_fence(t)
        slo_gate = [slo_pairs[1][1], slo_pairs[2][1], slo_pairs[3][1], n_of]
        y_all = [f"y{c}" for c in range(CIN)]
        for dc in range(DC):
            sl = slo[dc // 4]
            pb, pbn = (pb_d0, "pbd0") if dc % 2 == 0 else (pb_d1, "pbd1")

            def mmo(T, dc=dc, sl=sl, pb=pb):
                ins = None
                for c in range(CIN):
                    ins = T.matmul(pb.ap(), tile(sl, (dc % 4) * 4 + c),
                                   y_bf.ap()[:, W * c:W * (c + 1)],
                                   start=(c == 0), stop=(c == CIN - 1))
                return ins

            pe(mmo, [f"ring{sl}", f"GQ{slo_gate[dc // 4]}"] + y_all, [pbn], t)
            vec(lambda v, dc=dc, pb=pb, pbn=pbn: v.tensor_tensor(
                out=Rc(dc), in0=Rc(dc), in1=pb.ap(), op=ALU.add),
                [pbn, f"R{dc}"], [f"R{dc}"], t)

    for w_ in range(waves):
        rl = []
        for dc in range(DC):
            src = xT[128 * dc:128 * (dc + 1), W * w_:W * (w_ + 1)]
            n = sq_state["n"]
            sq_state["n"] += 1
            s.dma("sync", lambda e, d=Rc(dc), sp=src: e.dma_start(out=d, in_=sp),
                  writes=[f"R{dc}", f"SQ{n}"], tag=f"w{w_}ld")
            rl.append(n)
        rgate = rl[1:] + [sync_fence(f"w{w_}ld")]
        for dc in range(DC):
            vec(lambda v: v.tensor_copy(out=stwT.ap()[:, 0:128], in_=ident.ap()),
                [f"R{dc}", f"SQ{rgate[dc]}", "ident"], [f"R{dc}", "stwT"],
                f"w{w_}ld")
        ffn(w_, 1)
        if debug and w_ == 0:
            s.dma("scalar", lambda e: e.dma_start(out=dr1, in_=Rt.ap()),
                  reads=[f"R{dc}" for dc in range(DC)], tag="dbg")
        mamba(w_)
        if debug and w_ == 0:
            s.dma("scalar", lambda e: e.dma_start(out=drm, in_=Rt.ap()),
                  reads=[f"R{dc}" for dc in range(DC)], tag="dbg")
        ffn(w_, 2)
        for dc in range(DC - 4, DC):
            vec(lambda v: v.tensor_copy(out=stwT.ap()[:, 0:128], in_=ident.ap()),
                [f"R{dc}", "ident"], [f"Rs{dc}", "stwT"], f"w{w_}st")
        for dc in range(DC):
            dst = yT[128 * dc:128 * (dc + 1), W * w_:W * (w_ + 1)]
            rd = [f"R{dc}"] + ([f"Rs{dc}"] if dc >= DC - 4 else [])
            s.dma("sync", lambda e, d=dst, sp=Rc(dc): e.dma_start(out=d, in_=sp),
                  reads=rd, tag=f"w{w_}st")

    if light:
        lful = {}
        for nm, src, nsl, wid in (("lqg", qg3, OC, 2048), ("lqu", qu3, OC, 2048),
                                  ("lqd", qd3, DC, 8192)):
            o = nc.dram_tensor(nm, [nsl, 128, wid], BF16, kind="ExternalOutput").ap()
            lful[nm] = o
            for sl in range(nsl):
                s.dma("scalar", lambda e, o=o, src=src, sl=sl: e.dma_start(
                    out=o[sl], in_=src[sl]),
                    reads=[f"q{nm[2]}_h{sl // 4}"], tag="light")
        lrp0 = nc.dram_tensor("lrp0", [128, 512], F32, kind="ExternalOutput").ap()
        lacc = nc.dram_tensor("lacc", [128, 1], F32, kind="ExternalOutput").ap()
        s.dma("scalar", lambda e: e.dma_start(out=lrp0, in_=rp[0].ap()),
              reads=["rp0"], tag="light")
        s.dma("scalar", lambda e: e.dma_start(out=lacc, in_=acc0.ap()),
              reads=["acc0"], tag="light")
        lq = nc.dram_tensor("lq", [128, 2048], BF16, kind="ExternalOutput").ap()
        lcols = {n: nc.dram_tensor(f"lcol_{n}", [128, 1], F32, kind="ExternalOutput").ap()
                 for n in ("wm_g", "wm_u", "wm_d", "th_d", "nth")}
        s.dma("scalar", lambda e: e.dma_start(out=lq, in_=qg3[0]),
              reads=["qg_h0"], tag="light")
        for m in ("g", "u", "d"):
            s.dma("scalar", lambda e, m=m: e.dma_start(out=lcols[f"wm_{m}"],
                                                       in_=wm[m].ap()),
                  reads=[f"wm_{m}"], tag="light")
        s.dma("scalar", lambda e: e.dma_start(out=lcols["th_d"], in_=sw["d"].ap()),
              reads=["sw_d"], tag="light")
        s.dma("scalar", lambda e: e.dma_start(out=lcols["nth"], in_=pcol.ap()),
              reads=["pcol"], tag="light")
    s.barrier_all(lambda e: e.nop())

    import contextlib
    with contextlib.ExitStack() as stack:
        sems = {}
        for e in Sched.CSEMS:
            sems[("c", e)] = stack.enter_context(nc.semaphore(f"sc_{e}"))
        for q in Sched.QSEMS:
            sems[("d", q)] = stack.enter_context(nc.semaphore(f"sd_{q}"))
        block = stack.enter_context(nc.Block())

        def mk_section(name):
            def run(eng):
                satisfied = {}
                for waits, fn, key, inc, tag in s.ops[name]:
                    for k in sorted(waits.keys(), key=str):
                        v = waits[k]
                        if satisfied.get(k, 0) < v:
                            eng.wait_ge(sems[k], v)
                            satisfied[k] = v
                    ins = fn(eng)
                    if key is not None:
                        ins.then_inc(sems[key], inc)
            return run

        block.tensor(mk_section("tensor"))
        block.vector(mk_section("vector"))
        block.scalar(mk_section("scalar"))
        block.gpsimd(mk_section("gpsimd"))
        block.sync(mk_section("sync"))

    return nc


_NC_CACHE = []


def _const_patterns():
    f = np.float32
    ident = np.eye(128, dtype=f)
    oh = np.zeros((128, 512), f)
    for sl in range(4):
        oh[sl, 128 * sl:128 * (sl + 1)] = 1.0
    eb = np.zeros((H, CIN * 128), f)
    for k in range(CIN):
        for p in range(H):
            lo = 16 * p - 128 * k
            if 0 <= lo < 128:
                eb[p, 128 * k + lo:128 * k + lo + 16] = 1.0
    return ident, oh, eb


def kernel(L0_attn_hidden_states, L0_attn_W_in_x, L0_attn_W_in_z, L0_attn_conv_w,
           L0_attn_conv_b, L0_attn_W_param, L0_attn_W_dt, L0_attn_b_dt, L0_attn_A_log,
           L0_attn_D, L0_attn_W_out, L0_ffn_w_gate, L0_ffn_w_up, L0_ffn_w_down,
           L0_ffn_ln_weight):
    B = L0_attn_hidden_states.shape[0]
    f = np.float32

    def c(a):
        return np.ascontiguousarray(np.asarray(a, dtype=f))

    wgT = c(np.asarray(L0_ffn_w_gate, f).T)
    wuT = c(np.asarray(L0_ffn_w_up, f).T)
    wdT = c(np.asarray(L0_ffn_w_down, f).T)

    def blocks_kc(aT):
        K_, N_ = aT.shape
        a4 = aT.reshape(K_ // 128, 128, N_ // 128, 128)
        return np.ascontiguousarray(a4.transpose(2, 1, 0, 3).reshape(N_ // 128, 128, K_))

    wxT4 = blocks_kc(np.asarray(L0_attn_W_in_x, f).T)
    wzT4 = blocks_kc(np.asarray(L0_attn_W_in_z, f).T)

    def tiles_flat(aT):
        K_, N_ = aT.shape
        a4 = aT.reshape(K_ // 128, 128, N_ // 128, 128)
        return a4.transpose(2, 0, 1, 3).reshape(-1, 128, 128)

    wp_flat = tiles_flat(np.asarray(L0_attn_W_param, f).T)
    wp_pad = np.zeros((48, 128, 128), f)
    wp_pad[:36] = wp_flat
    wpS = np.ascontiguousarray(
        wp_pad.reshape(3, 16, 128, 128).transpose(0, 2, 1, 3).reshape(3, 128, 2048))
    wo_flat = tiles_flat(np.asarray(L0_attn_W_out, f).T)
    woS = np.ascontiguousarray(
        wo_flat.reshape(4, 16, 128, 128).transpose(0, 2, 1, 3).reshape(4, 128, 2048))

    shared = {
        "wgT": wgT, "wuT": wuT, "wdT": wdT, "wxT4": wxT4, "wzT4": wzT4,
        "wpS": wpS, "wdtT": c(np.asarray(L0_attn_W_dt, f).T), "woS": woS,
        "convw": c(np.asarray(L0_attn_conv_w, f).reshape(DIN, 4)),
        "convb": c(L0_attn_conv_b), "Dvec": c(L0_attn_D), "bdt": c(L0_attn_b_dt),
        "Alog": c(L0_attn_A_log), "lnw": c(L0_ffn_ln_weight),
    }
    shared["identI"], shared["oh128I"], shared["Eb4I"] = _const_patterns()
    in_maps = []
    for b in range(B):
        m = dict(shared)
        m["xT"] = c(np.asarray(L0_attn_hidden_states[b], f).T)
        in_maps.append(m)

    if not _NC_CACHE:
        _NC_CACHE.append(build_program())
    nc = _NC_CACHE[0]
    res = run_bass_kernel_spmd(nc, in_maps, core_ids=list(range(8)))
    out = np.stack([np.asarray(r["yT"], dtype=np.float32).T for r in res.results],
                   axis=0)
    return out
